# version 03
# baseline (speedup 1.0000x reference)
import math
import numpy as np
import ml_dtypes
import concourse.bass as bass
import concourse.mybir as mybir
from concourse.bass_utils import run_bass_kernel_spmd

F32 = mybir.dt.float32
BF16 = mybir.dt.bfloat16
AF = mybir.ActivationFunctionType
ALU = mybir.AluOpType
AX = mybir.AxisListType.X

T = 2048
D = 4096
NCH = 32
KVH = 8
TTILES = 16
NBLK = 4
SCL = 1.0 / math.sqrt(128.0)
EPS = 1e-5
NEG = -1.0e30

DEBUG = False


def _consts():
    inv_freq = (1.0 / (np.float32(10000.0) **
                       (np.arange(0, 128, 2, dtype=np.float32) / np.float32(128)))).astype(np.float32)
    t = np.arange(T, dtype=np.float32)
    freqs = (t[:, None] * inv_freq[None, :]).astype(np.float32)
    cos = np.cos(freqs).astype(np.float32).T
    sin = np.sin(freqs).astype(np.float32).T
    s1 = np.float32(128.0 ** -0.25)
    cos_t = (np.concatenate([cos, cos], axis=0) * s1).astype(np.float32)
    sincross = (np.concatenate([-sin, sin], axis=0) * s1).astype(np.float32)
    ident = np.eye(128, dtype=np.float32).astype(ml_dtypes.bfloat16)
    j = np.arange(128)[None, :]
    p = np.arange(128)[:, None]
    mask = np.zeros((128, 256), dtype=np.float32)
    mask[:, 0:128] = np.where(j < p, NEG, 0.0)
    mask[:, 128:256] = np.where(j > p, NEG, 0.0)
    return cos_t, sincross, ident, mask


def build_program():
    nc = bass.Bass("TRN2", target_bir_lowering=False)

    d_x = nc.dram_tensor("x", [T, D], F32, kind="ExternalInput").ap()
    d_wq = nc.dram_tensor("wq", [D, D], F32, kind="ExternalInput").ap()
    d_wk = nc.dram_tensor("wk", [D, 1024], F32, kind="ExternalInput").ap()
    d_wv = nc.dram_tensor("wv", [D, 1024], F32, kind="ExternalInput").ap()
    d_wo = nc.dram_tensor("wo", [D, D], F32, kind="ExternalInput").ap()
    d_wup = nc.dram_tensor("wup", [D, D], F32, kind="ExternalInput").ap()
    d_gw = nc.dram_tensor("gw", [4, D], F32, kind="ExternalInput").ap()
    d_eb = nc.dram_tensor("eb", [1, 4], F32, kind="ExternalInput").ap()
    d_la = nc.dram_tensor("la", [32, D], F32, kind="ExternalInput").ap()
    d_lb = nc.dram_tensor("lb", [4, D, 8], F32, kind="ExternalInput").ap()
    d_cos = nc.dram_tensor("cos_t", [128, T], F32, kind="ExternalInput").ap()
    d_sx = nc.dram_tensor("sincross", [128, T], F32, kind="ExternalInput").ap()
    d_id = nc.dram_tensor("ident", [128, 128], BF16, kind="ExternalInput").ap()
    d_msk = nc.dram_tensor("maskfd", [128, 256], F32, kind="ExternalInput").ap()
    d_cst = nc.dram_tensor("cstv", [128, 8], F32, kind="ExternalInput").ap()
    d_out = nc.dram_tensor("out", [T, D], F32, kind="ExternalOutput").ap()

    def scratch(name, shape, dt):
        kind = "ExternalOutput" if DEBUG else "Internal"
        return nc.dram_tensor(name, shape, dt, kind=kind).ap()

    s_ln1 = scratch("s_ln1", [T, D], BF16)
    s_qraw = scratch("s_qraw", [NCH, 128, T], BF16)
    s_kraw = scratch("s_kraw", [8, 128, T], BF16)
    s_q = scratch("s_q", [TTILES, 128, D], BF16)
    s_v = scratch("s_v", [TTILES, 128, 1024], BF16)
    s_wvb = scratch("s_wvb", [NCH, 128, 1024], BF16)
    s_wqb = scratch("s_wqb", [NCH, 128, NCH, 128], BF16)
    s_wkb = scratch("s_wkb", [8, 128, NCH, 128], BF16)
    s_wob = scratch("s_wob", [NCH, 128, NCH, 128], BF16)
    s_wub = scratch("s_wub", [D, D], BF16)
    s_acat = scratch("s_acat", [48, D], BF16)
    s_wo_fm = scratch("s_wo_fm", [NCH, 128, T], BF16)
    s_x1 = scratch("s_x1", [T, D], F32)
    s_ln3 = scratch("s_ln3", [T, D], BF16)
    s_y = scratch("s_y", [NCH, 128, T], BF16)
    if DEBUG:
        s_dsc = scratch("s_dsc", [4, 128, 640], F32)
        s_dpr = scratch("s_dpr", [128, 4, 640], BF16)
        s_dst = scratch("s_dst", [128, 64], F32)
        s_datt = scratch("s_datt", [128, NCH, 512], BF16)
        s_dzmt = scratch("s_dzmt", [32, T], BF16)
        s_dbct = scratch("s_dbct", [32, D], BF16)
        s_dact = scratch("s_dact", [128, NCH * 48], BF16)
        s_dzl = scratch("s_dzl", [128, 64], F32)

    A = nc.alloc_sbuf_tensor_at

    def region(size):
        st, en = nc.bump_sbuf(size)
        return st

    u1 = region(131072)
    u2 = region(32768)
    uw = region(16384)
    u5 = region(28672)
    up = region(2048)

    sb_id = A("c_id", [128, 128], BF16, offset=up)
    sb_msk = A("c_msk", [128, 256], F32, offset=up + 256)
    sb_st = A("c_st", [128, 64], F32, offset=up + 1280)
    sb_cst = A("c_cst", [128, 8], F32, offset=up + 1536)
    nc.const_aps.register(F32, EPS, sb_cst.ap()[:, 0:1])
    nc.const_aps.register(F32, -0.5, sb_cst.ap()[:, 1:2])
    nc.const_aps.register(F32, 1.0 / D, sb_cst.ap()[:, 2:3])
    nc.const_aps.register(F32, SCL, sb_cst.ap()[:, 3:4])

    ln1f = A("ln1f", [128, NCH, T], BF16, offset=u1)
    ln3f = A("ln3f", [128, NCH, T], BF16, offset=u1)
    attnT = A("attnT", [128, NCH, 512], BF16, offset=u1)
    v_sb = A("v_sb", [128, TTILES, 1024], BF16, offset=u1 + 32768)
    qt_sb = [A(f"qt{i}", [128, NCH, 128], BF16, offset=u1 + 65536 + i * 8192)
             for i in range(2)]
    probs = A("probs", [128, 4, 640], BF16, offset=u1 + 81920)
    probsT = A("probsT", [128, 4, 640], BF16, offset=u1 + 87040)
    wo_blk = [A(f"woblk{i}", [128, NCH, 128], BF16, offset=u1 + 92160 + i * 8192)
              for i in range(2)]
    xt_sb = [A(f"xt{i}", [128, D], F32, offset=u1 + i * 16384) for i in range(2)]
    sq_sb = A("sqsc", [128, D], BF16, offset=u1 + 32768)
    lnout_sb = [A(f"lno{i}", [128, D], BF16, offset=u1 + 40960 + i * 8192)
                for i in range(2)]
    wot_sb = [A(f"wot{i}", [128, D], BF16, offset=u1 + 57344 + i * 8192)
              for i in range(2)]
    f32a_sb = A("f32a", [128, D], F32, offset=u1 + 73728)
    f32b_sb = [A(f"f32b{i}", [128, D], F32, offset=u1 + 90112 + i * 16384)
               for i in range(2)]
    dbg_sb = A("dbgsc", [128, 640], F32, offset=u1 + 122880)

    k_fm = A("k_fm", [128, KVH, T], BF16, offset=u2)
    wup_blk = A("wupblk", [128, NCH, 512], BF16, offset=u2)

    wblk = [A(f"wblk{i}", [128, NCH, 128], BF16, offset=uw + i * 8192)
            for i in range(2)]

    wv_sb = [A(f"wv{i}", [128, 1024], BF16, offset=u5 + i * 2048) for i in range(2)]
    qp_sb = [A(f"qp{i}", [128, 512], BF16, offset=u5 + i * 1024) for i in range(2)]
    qs_sb = [A(f"qs{i}", [128, 512], BF16, offset=u5 + 2048 + i * 1024) for i in range(2)]
    stg = [A(f"stg{i}", [128, 512], BF16, offset=u5 + 4096 + i * 1024) for i in range(4)]
    ta_sb = A("ta", [128, 512], F32, offset=u5 + 8192)
    tc_sb = A("tc", [128, 512], F32, offset=u5 + 10240)
    cos_sb = A("cosf", [128, T], F32, offset=u5 + 12288)
    sx_sb = A("sxf", [128, T], F32, offset=u5 + 20480)
    stg8 = [A(f"st8{i}", [128, 512], BF16, offset=u5 + i * 1024) for i in range(8)]
    sb_bct = A("bcat", [32, D], BF16, offset=u5)
    sb_act = A("acatT", [128, NCH, 48], BF16, offset=u5 + 8192)
    sb_zmt = A("zmT", [32, T], BF16, offset=u5 + 11264)
    sb_ebb = A("ebb", [128, 4], F32, offset=u5 + 15360)
    bctT_sb = A("bctT", [128, 32], BF16, offset=u5 + 15392)
    zl_sb = A("zls", [128, 64], F32, offset=u5 + 15488)
    zm_sb = A("zms", [128, 32], BF16, offset=u5 + 15744)
    stg6 = [A(f"st6{i}", [128, 512], BF16, offset=u5 + 16384 + i * 1024)
            for i in range(8)]

    PS = nc.alloc_psum_tensor("PSALL", [128, 4096], F32)
    ps8 = [PS.ap()[:, i * 512:(i + 1) * 512] for i in range(8)]
    sc_ps = [PS.ap()[:, 0:640], PS.ap()[:, 1024:1664]]
    av_ps = [PS.ap()[:, 2048:2560], PS.ap()[:, 2560:3072]]
    t_bf = [PS.ap()[:, 3072:3584].bitcast(BF16), PS.ap()[:, 3584:4096].bitcast(BF16)]
    z_ps = PS.ap()[:, 2048:2084]

    ENG = ("sync", "tensor", "vector", "scalar", "gpsimd")
    ops = {e: [] for e in ENG}
    tot = {}
    S = {}

    def inc(sem, amt=1):
        tot[sem] = tot.get(sem, 0) + amt
        return (sem, tot[sem])

    def cur(sem):
        return (sem, tot.get(sem, 0))

    def W(e, tok):
        if tok is None:
            return
        sem, val = tok
        if val <= 0:
            return
        ops[e].append(lambda eng: eng.wait_ge(S[sem], val))

    nbar = [0]

    def barrier():
        nbar[0] += 1
        tgt = len(ENG) * nbar[0]
        for e in ENG:
            if e != "sync":
                ops[e].append(lambda eng: eng.drain())
            ops[e].append(lambda eng: eng.nop().then_inc(S["bar"], 1))
        for e in ENG:
            ops[e].append(lambda eng, t=tgt: eng.wait_ge(S["bar"], t))

    QSEM = {"sync": "qs", "scalar": "qa", "gpsimd": "qg"}

    def dma(q, out, in_, **kw):
        sem = QSEM[q]
        tok = inc(sem, 16)
        ops[q].append(lambda eng, o=out, i=in_, k=dict(kw):
                      eng.dma_start(out=o, in_=i, **k).then_inc(S[sem], 16))
        return tok

    def mm(out, lhsT, rhs, start, stop, track=False):
        if track:
            tok = inc("pe")
            ops["tensor"].append(lambda eng, o=out, l=lhsT, r=rhs, a=start, b=stop:
                                 eng.matmul(o, l, r, start=a, stop=b).then_inc(S["pe"], 1))
            return tok
        ops["tensor"].append(lambda eng, o=out, l=lhsT, r=rhs, a=start, b=stop:
                             eng.matmul(o, l, r, start=a, stop=b))
        return None

    def tr(out, in_, track=False):
        if track:
            tok = inc("pe")
            ops["tensor"].append(lambda eng, o=out, i=in_:
                                 eng.transpose(o, i, sb_id.ap()).then_inc(S["pe"], 1))
            return tok
        ops["tensor"].append(lambda eng, o=out, i=in_:
                             eng.transpose(o, i, sb_id.ap()))
        return None

    def ve(fn, track=True):
        if track:
            tok = inc("ve")
            ops["vector"].append(lambda eng, f=fn: f(eng).then_inc(S["ve"], 1))
            return tok
        ops["vector"].append(lambda eng, f=fn: f(eng))
        return None

    def ac(fn, track=True):
        if track:
            tok = inc("ac")
            ops["scalar"].append(lambda eng, f=fn: f(eng).then_inc(S["ac"], 1))
            return tok
        ops["scalar"].append(lambda eng, f=fn: f(eng))
        return None

    dma("sync", sb_id.ap(), d_id)
    dma("sync", sb_msk.ap(), d_msk)
    t_cst = dma("sync", sb_cst.ap(), d_cst)

    def stage_mcmajor(dst, src, n_mc):
        for c in range(NCH):
            i_ap = src[c * 128:(c + 1) * 128, :].rearrange("p (a m) -> p a m", m=128)
            o_ap = dst[:, :, c, :].rearrange("a p m -> p a m")
            for piece in range(2):
                lo = piece * (n_mc // 2)
                hi = (piece + 1) * (n_mc // 2)
                dma("gpsimd", o_ap[:, lo:hi, :], i_ap[:, lo:hi, :])

    stage_mcmajor(s_wkb, d_wk, 8)
    t_st_wk = cur("qg")
    stage_mcmajor(s_wqb, d_wq, NCH)
    t_st_wq = cur("qg")
    for c in range(NCH):
        dma("gpsimd", s_wvb[c], d_wv[c * 128:(c + 1) * 128, :])
    t_st_wv = cur("qg")
    stage_mcmajor(s_wob, d_wo, NCH)
    t_st_wo = cur("qg")
    for c in range(NCH):
        dma("gpsimd", s_wub[c * 128:(c + 1) * 128, :], d_wup[c * 128:(c + 1) * 128, :])
    t_st_wu = cur("qg")
    dma("gpsimd", s_acat[0:32, :], d_la)
    dma("gpsimd", s_acat[32:36, :], d_gw)
    t_st_ac = cur("qg")

    def ln_stats_apply(src_ap, cols, out_ap, out2=None, src2=None):
        st_ = sb_st.ap()
        negm = st_[:, cols:cols + 1]
        lnv = st_[:, cols + 1:cols + 2]
        rstd = st_[:, cols + 2:cols + 3]
        b2 = st_[:, cols + 3:cols + 4]
        tmp8 = st_[:, 40:48]
        t1 = ve(lambda e: e.reduce_sum(tmp8, src_ap.rearrange("p (a b) -> p a b", b=512),
                                       axis=AX))
        W("vector", t1)
        t1b = ve(lambda e: e.reduce_sum(negm, tmp8, axis=AX))
        W("vector", t1b)
        t2 = ve(lambda e: e.tensor_scalar_mul(negm, negm, -1.0 / D))
        W("scalar", t2)
        ac(lambda e: e.activation(out=sq_sb.ap(), in_=src_ap, func=AF.Square,
                                  bias=negm, scale=1.0, accum_out=lnv), track=False)
        ac(lambda e: e.activation(out=lnv, in_=lnv, func=AF.Ln,
                                  bias=EPS, scale=1.0 / D), track=False)
        t5 = ac(lambda e: e.activation(out=rstd, in_=lnv, func=AF.Exp,
                                       bias=0.0, scale=-0.5))
        W("vector", t5)
        t6 = ve(lambda e: e.tensor_mul(b2, negm, rstd))
        W("scalar", t6)
        t7 = ac(lambda e: e.activation(out=out_ap, in_=src_ap,
                                       func=AF.Identity, bias=b2, scale=rstd))
        return t1, t7

    W("scalar", t_cst)
    t_x = [None, None]
    t_x[0] = dma("sync", xt_sb[0].ap(), d_x[0:128, :])
    st1 = []
    prev_apply = None
    for i in range(TTILES):
        b = i % 2
        if i + 1 < TTILES:
            W("sync", prev_apply)
            t_x[(i + 1) % 2] = dma("sync", xt_sb[(i + 1) % 2].ap(),
                                   d_x[(i + 1) * 128:(i + 2) * 128, :])
        W("vector", t_x[b])
        if i >= 2:
            W("vector", st1[i - 2])
            W("scalar", st1[i - 2])
        if prev_apply is not None:
            W("vector", prev_apply)
        _, t7 = ln_stats_apply(xt_sb[b].ap(), 0, lnout_sb[b].ap())
        prev_apply = t7
        W("sync", t7)
        st1.append(dma("sync", s_ln1[i * 128:(i + 1) * 128, :], lnout_sb[b].ap()))
    barrier()

    W("sync", st1[-1])
    for c in range(NCH):
        t_ln1f = dma("sync", ln1f.ap()[:, c, :], s_ln1[:, c * 128:(c + 1) * 128],
                     transpose=True)
    dma("sync", cos_sb.ap(), d_cos)
    t_tab = dma("sync", sx_sb.ap(), d_sx)
    barrier()

    EV = {}

    rope_pend = []
    rope_done = {}
    st_ro = {}
    rope_idx = [0]
    st_q = []

    def rope_sched(kind, mc, nb, st_tok):
        i = rope_idx[0]
        rope_idx[0] += 1
        raw = s_qraw if kind == "q" else s_kraw
        W("scalar", st_tok)
        W("scalar", rope_done.get(i - 2))
        tk0 = nb * 512
        dma("scalar", qp_sb[i % 2].ap(), raw[mc, :, tk0:tk0 + 512])
        dma("scalar", qs_sb[i % 2].ap()[0:64, :], raw[mc, 64:128, tk0:tk0 + 512])
        tok = dma("scalar", qs_sb[i % 2].ap()[64:128, :], raw[mc, 0:64, tk0:tk0 + 512])
        rope_pend.append((i, kind, mc, nb, tok))
        while len(rope_pend) > 2:
            rope_emit(rope_pend.pop(0))

    def rope_drain():
        while rope_pend:
            rope_emit(rope_pend.pop(0))

    def rope_emit(item):
        i, kind, mc, nb, tok = item
        tk0 = nb * 512
        cs = cos_sb.ap()[:, tk0:tk0 + 512]
        sx = sx_sb.ap()[:, tk0:tk0 + 512]
        W("vector", tok)
        ve(lambda e: e.tensor_mul(ta_sb.ap(), qp_sb[i % 2].ap(), cs), track=False)
        ve(lambda e: e.tensor_mul(tc_sb.ap(), qs_sb[i % 2].ap(), sx), track=False)
        if kind == "k":
            t = ve(lambda e, mc=mc, tk0=tk0: e.tensor_add(
                k_fm.ap()[:, mc, tk0:tk0 + 512], ta_sb.ap(), tc_sb.ap()))
            rope_done[i] = t
        else:
            j = 2 + i % 2
            W("vector", st_ro.get(i - 2))
            t = ve(lambda e, j=j: e.tensor_add(stg[j].ap(), ta_sb.ap(), tc_sb.ap()))
            rope_done[i] = t
            W("sync", t)
            dst = s_q.rearrange("r p (c t) -> p r c t", c=NCH)[:, nb * 4:(nb + 1) * 4, mc, :]
            st = dma("sync", dst, stg[j].ap().rearrange("p (r t) -> p r t", r=4))
            st_ro[i] = st
            st_q.append(st)

    def qk_gemm(n_mc, wsrc, evac_fn, stage_tok):
        t_w = [None, None]
        W("sync", stage_tok)
        t_w[0] = dma("sync", wblk[0].ap().rearrange("p c m -> p (c m)"),
                     wsrc(0).rearrange("p c m -> p (c m)"))
        pe_done = {}
        ev_done = EV
        for mc in range(n_mc):
            wb = wblk[mc % 2].ap()
            if mc + 1 < n_mc:
                W("sync", pe_done.get(mc - 1))
                t_w[(mc + 1) % 2] = dma("sync",
                                        wblk[(mc + 1) % 2].ap().rearrange("p c m -> p (c m)"),
                                        wsrc(mc + 1).rearrange("p c m -> p (c m)"))
            slots = [ps8[4 * (mc % 2) + j] for j in range(4)]
            W("tensor", t_w[mc % 2])
            for j in range(4):
                W("tensor", ev_done.get(4 * (mc % 2) + j))
            stop_toks = [None] * 4
            for kc in range(NCH):
                for nb in range(NBLK):
                    trk = (kc == NCH - 1)
                    tok = mm(slots[nb], wb[:, kc, :], ln1f.ap()[:, kc, nb * 512:(nb + 1) * 512],
                             start=(kc == 0), stop=trk, track=trk)
                    if trk:
                        stop_toks[nb] = tok
            pe_done[mc] = stop_toks[3]
            for nb in range(NBLK):
                W("vector", stop_toks[nb])
                ev_done[4 * (mc % 2) + nb] = evac_fn(mc, nb, slots[nb])
        return pe_done

    W("vector", t_tab)
    W("tensor", t_ln1f)
    raw_idx = [0]
    st_raw = {}

    def raw_evac(kind):
        raw = s_qraw if kind == "q" else s_kraw

        def evac(mc, nb, ps):
            i = raw_idx[0]
            raw_idx[0] += 1
            j = i % 2
            W("vector", st_raw.get(i - 2))
            t = ve(lambda e, j=j, ps=ps: e.tensor_copy(stg[j].ap(), ps))
            W("sync", t)
            st = dma("sync", s_raw_dst(raw, mc, nb), stg[j].ap())
            st_raw[i] = st
            rope_sched(kind, mc, nb, st)
            return t
        return evac

    def s_raw_dst(raw, mc, nb):
        return raw[mc, :, nb * 512:(nb + 1) * 512]

    qk_gemm(8, lambda mc: s_wkb[mc], raw_evac("k"), t_st_wk)

    qk_gemm(32, lambda mc: s_wqb[mc], raw_evac("q"), t_st_wq)
    rope_drain()

    t_wv = [None, None]
    W("sync", t_st_wv)
    W("sync", rope_done.get(rope_idx[0] - 1))
    W("sync", rope_done.get(rope_idx[0] - 2))
    t_wv[0] = dma("sync", wv_sb[0].ap(), s_wvb[0])
    st_v = []
    v_pe_done = {}
    for ts in range(8):
        last_tok = None
        for kc in range(NCH):
            gi = ts * NCH + kc
            if gi + 1 < 8 * NCH:
                W("sync", v_pe_done.get(gi - 1))
                t_wv[(gi + 1) % 2] = dma("sync", wv_sb[(gi + 1) % 2].ap(),
                                         s_wvb[(kc + 1) % NCH])
            W("tensor", t_wv[gi % 2])
            if kc == 0:
                for j in range(4):
                    W("tensor", EV.get(4 * (ts % 2) + j))
            for tl in range(2):
                tt = ts * 2 + tl
                for half in range(2):
                    trk = (kc == NCH - 1)
                    tok = mm(ps8[4 * (ts % 2) + 2 * tl + half],
                             ln1f.ap()[:, kc, tt * 128:(tt + 1) * 128],
                             wv_sb[gi % 2].ap()[:, half * 512:(half + 1) * 512],
                             start=(kc == 0), stop=trk,
                             track=(half == 1 and tl == 1))
                    if tok is not None:
                        last_tok = tok
            v_pe_done[gi] = last_tok
        for tl in range(2):
            tt = ts * 2 + tl
            W("vector", last_tok)
            if ts == 0 and tl == 0 and st_q:
                W("vector", st_q[-1])
            if len(st_v) >= 4:
                W("vector", st_v[len(st_v) - 4])
                W("vector", st_v[len(st_v) - 3])
            sga = stg[tl * 2].ap()
            sgb = stg[tl * 2 + 1].ap()
            ve(lambda e, a=4 * (ts % 2) + 2 * tl, o=sga: e.tensor_copy(o, ps8[a]), track=False)
            c2 = ve(lambda e, a=4 * (ts % 2) + 2 * tl + 1, o=sgb: e.tensor_copy(o, ps8[a]))
            EV[4 * (ts % 2) + 2 * tl] = c2
            EV[4 * (ts % 2) + 2 * tl + 1] = c2
            W("sync", c2)
            st_v.append(dma("sync", s_v[tt, :, 0:512], sga))
            st_v.append(dma("sync", s_v[tt, :, 512:1024], sgb))
    barrier()

    W("sync", st_v[-1])
    for tt in range(TTILES):
        t_vl = dma("sync", v_sb.ap()[:, tt, :], s_v[tt])
    W("sync", st_q[-1])
    t_qt = [None, None]
    t_qt[0] = dma("sync", qt_sb[0].ap().rearrange("p c t -> p (c t)"), s_q[0])

    sc_free = {}
    probs_free = {}
    exp_tok = {}
    rc_tok = {}
    pT_cp = {}
    pT_used = {}
    av_free = {}
    qk_last = {}
    wo_last = None
    st_wo = []
    wo_ev = {}
    sc_rot = [0]
    pt_rot = [0]
    av_rot = [0]

    for quarter in range(4):
        att_evac_last = None
        if quarter > 0:
            for j in range(8):
                W("tensor", wo_ev.get(j))
        first_evac_of_quarter = True
        for rr in range(4):
            r = quarter * 4 + rr
            b = r % 2
            if r + 1 < TTILES:
                W("sync", qk_last.get(r - 1))
                t_qt[(r + 1) % 2] = dma("sync",
                                        qt_sb[(r + 1) % 2].ap().rearrange("p c t -> p (c t)"),
                                        s_q[r + 1])
            nk = min(r, 4) + 1
            g0 = r - nk + 1
            nwin = nk * 128
            for h in range(KVH):
                t_pc4 = None
                unit_idx = r * KVH + h
                upar = unit_idx % 2
                nm_tok = {}

                def softmax_tail(g):
                    col_ = 4 * upar + g
                    ssum_ = sb_st.ap()[:, 24 + col_:25 + col_]
                    srec_ = sb_st.ap()[:, 32 + col_:33 + col_]
                    W("vector", exp_tok[(g, upar)])
                    t_rc = ve(lambda e: e.reciprocal(srec_, ssum_))
                    rc_tok[(g, upar)] = t_rc
                    W("vector", t_rc)
                    t_nr = ve(lambda e, g=g: e.tensor_scalar_mul(
                        probs.ap()[:, g, 0:nwin], probs.ap()[:, g, 0:nwin], srec_))
                    tpar = pt_rot[0] % 2
                    pst = t_bf[tpar]
                    pt_rot[0] += 1
                    W("tensor", t_nr)
                    W("tensor", pT_cp.get(tpar))
                    for c in range(nk):
                        t_tp = tr(pst[:, c * 128:(c + 1) * 128],
                                  probs.ap()[:, g, c * 128:(c + 1) * 128],
                                  track=(c == nk - 1))
                    probs_free[g] = t_tp
                    W("scalar", t_tp)
                    W("scalar", pT_used.get(g))
                    t_pc = ac(lambda e, pst=pst, g=g: e.copy(
                        probsT.ap()[:, g, 0:nwin], pst[:, 0:nwin]))
                    pT_cp[tpar] = t_pc
                    return t_pc

                for g in range(4):
                    hq = h * 4 + g
                    par = sc_rot[0] % 2
                    sps = sc_ps[par]
                    sc_rot[0] += 1
                    W("tensor", t_qt[b])
                    if r == 0 and h == 0 and g == 0:
                        W("tensor", t_vl)
                    W("tensor", sc_free.get(par))
                    lhsT = qt_sb[b].ap()[:, hq, :]
                    rhs = k_fm.ap()[:, h, g0 * 128:g0 * 128 + nwin]
                    if nwin > 512:
                        mm(sps[:, 0:512], lhsT, rhs[:, 0:512], start=True, stop=True)
                        t_sc = mm(sps[:, 512:nwin], lhsT, rhs[:, 512:nwin],
                                  start=True, stop=True, track=True)
                    else:
                        t_sc = mm(sps[:, 0:nwin], lhsT, rhs, start=True, stop=True,
                                  track=True)
                    qk_last[r] = t_sc
                    W("vector", t_sc)
                    W("vector", exp_tok.get((g, upar)))
                    if r >= 4:
                        mv = sps.rearrange("p (a c) -> p a c", c=128)[:, 0:5:4, :]
                        mk = sb_msk.ap().rearrange("p (a c) -> p a c", c=128)
                        ve(lambda e, mv=mv, mk=mk: e.tensor_tensor(mv, mv, mk, op=ALU.add),
                           track=False)
                    else:
                        sl = sps[:, (nk - 1) * 128:nwin]
                        ve(lambda e, sl=sl: e.tensor_tensor(sl, sl, sb_msk.ap()[:, 128:256],
                                                            op=ALU.add), track=False)
                    if DEBUG and r == 1 and h == 0:
                        t_d1 = ve(lambda e, sps=sps, g=g: e.tensor_copy(
                            dbg_sb.ap(), sps[:, 0:640]))
                        W("sync", t_d1)
                        dma("sync", s_dsc[g], dbg_sb.ap())
                    col = 4 * upar + g
                    smax = sb_st.ap()[:, 16 + col:17 + col]
                    ssum = sb_st.ap()[:, 24 + col:25 + col]
                    t_nm = ve(lambda e, sps=sps, n=nwin, sm=smax: e.reduce_max(
                        sm, sps[:, 0:n], axis=AX, negate=True))
                    nm_tok[g] = t_nm
                    if g > 0:
                        t_pc4 = softmax_tail(g - 1)
                    W("scalar", t_nm)
                    W("scalar", probs_free.get(g))
                    W("scalar", rc_tok.get((g, upar)))
                    t_ex = ac(lambda e, sps=sps, n=nwin, g=g, sm=smax, su=ssum: e.activation(
                        out=probs.ap()[:, g, 0:n], in_=sps[:, 0:n], func=AF.Exp,
                        bias=sm, scale=1.0, accum_out=su))
                    sc_free[par] = t_ex
                    exp_tok[(g, upar)] = t_ex
                t_pc4 = softmax_tail(3)
                apar = av_rot[0] % 2
                avp = av_ps[apar]
                av_rot[0] += 1
                W("tensor", t_pc4)
                W("tensor", av_free.get(apar))
                for c in range(nk):
                    gc = g0 + c
                    t_av = mm(avp.rearrange("p (g t) -> p g t", g=4),
                              v_sb.ap()[:, gc, h * 128:(h + 1) * 128],
                              probsT.ap()[:, :, c * 128:(c + 1) * 128],
                              start=(c == 0), stop=(c == nk - 1), track=(c == nk - 1))
                for g in range(4):
                    pT_used[g] = t_av
                if DEBUG and r == 1 and h == 0:
                    W("sync", t_pc4)
                    dma("sync", s_dpr, probs.ap())
                    dma("sync", s_dst, sb_st.ap())
                W("vector", t_av)
                if first_evac_of_quarter:
                    W("vector", wo_last)
                    first_evac_of_quarter = False
                dst = attnT.ap()[:, h * 4:(h + 1) * 4, rr * 128:(rr + 1) * 128]
                t_ae = ve(lambda e, avp=avp, dst=dst: e.tensor_copy(
                    dst, avp.rearrange("p (g t) -> p g t", g=4)))
                av_free[apar] = t_ae
                att_evac_last = t_ae
        if DEBUG and quarter == 0:
            W("sync", att_evac_last)
            dma("sync", s_datt.rearrange("p c t -> p (c t)"),
                attnT.ap().rearrange("p c t -> p (c t)"))
        W("tensor", sc_free.get(0))
        W("tensor", sc_free.get(1))
        W("tensor", pT_cp.get(0))
        W("tensor", pT_cp.get(1))
        W("tensor", av_free.get(0))
        W("tensor", av_free.get(1))
        t_wo = [None, None]
        if quarter == 0:
            W("sync", t_st_wo)
        wo_pe_done = {}

        def wo_src(mc):
            return s_wob[mc]

        t_wo[0] = dma("sync", wo_blk[0].ap().rearrange("p c m -> p (c m)"),
                      wo_src(0).rearrange("p c m -> p (c m)"))
        for mc in range(NCH):
            wb = wo_blk[mc % 2].ap()
            if mc + 1 < NCH:
                W("sync", wo_pe_done.get(mc - 1))
                t_wo[(mc + 1) % 2] = dma("sync",
                                         wo_blk[(mc + 1) % 2].ap().rearrange("p c m -> p (c m)"),
                                         wo_src(mc + 1).rearrange("p c m -> p (c m)"))
            slot = mc % 8
            W("tensor", t_wo[mc % 2])
            W("tensor", att_evac_last if mc == 0 else None)
            W("tensor", wo_ev.get(slot))
            for kc in range(NCH):
                trk = (kc == NCH - 1)
                tokw = mm(ps8[slot], wb[:, kc, :], attnT.ap()[:, kc, :],
                          start=(kc == 0), stop=trk, track=trk)
            wo_pe_done[mc] = tokw
            wo_last = tokw
            W("vector", tokw)
            if len(st_wo) >= 8:
                W("vector", st_wo[len(st_wo) - 8])
            sg = stg8[mc % 8].ap()
            t_evw = ve(lambda e, s=slot, o=sg: e.tensor_copy(o, ps8[s]))
            wo_ev[slot] = t_evw
            W("sync", t_evw)
            st_wo.append(dma("sync", s_wo_fm[mc, :, quarter * 512:(quarter + 1) * 512], sg))
    barrier()

    W("sync", st_wo[-1])
    t_wot = [None, None]
    t_x5 = [None, None]
    wo_cols = s_wo_fm.rearrange("c p t -> (c p) t")
    t_wot[0] = dma("sync", wot_sb[0].ap(), wo_cols[:, 0:128], transpose=True)
    t_x5[0] = dma("scalar", xt_sb[0].ap(), d_x[0:128, :])
    st5 = []
    prev15 = None
    prev8 = None
    for i in range(TTILES):
        b = i % 2
        if i + 1 < TTILES:
            W("sync", prev15)
            if i >= 1:
                W("scalar", st5[2 * (i - 1)])
            t_wot[(i + 1) % 2] = dma("sync", wot_sb[(i + 1) % 2].ap(),
                                     wo_cols[:, (i + 1) * 128:(i + 2) * 128], transpose=True)
            t_x5[(i + 1) % 2] = dma("scalar", xt_sb[(i + 1) % 2].ap(),
                                    d_x[(i + 1) * 128:(i + 2) * 128, :])
        W("vector", t_wot[b])
        if i >= 2:
            W("vector", st5[2 * (i - 2)])
            W("vector", st5[2 * (i - 2) + 1])
            W("scalar", st5[2 * (i - 2) + 1])
        if prev15 is not None:
            W("vector", prev15)
        _, t7 = ln_stats_apply(wot_sb[b].ap(), 4, f32a_sb.ap())
        W("vector", t7)
        W("vector", t_x5[b])
        if prev8 is not None:
            W("scalar", prev8)
        t8 = ve(lambda e, b=b: e.tensor_add(f32b_sb[b].ap(), xt_sb[b].ap(), f32a_sb.ap()))
        prev8 = t8
        _, t15 = ln_stats_apply(f32b_sb[b].ap(), 8, lnout_sb[b].ap())
        prev15 = t15
        W("sync", t8)
        st5.append(dma("sync", s_x1[i * 128:(i + 1) * 128, :], f32b_sb[b].ap()))
        W("sync", t15)
        st5.append(dma("sync", s_ln3[i * 128:(i + 1) * 128, :], lnout_sb[b].ap()))
    barrier()

    W("sync", st5[-1])
    for c in range(NCH):
        t_ln3f = dma("sync", ln3f.ap()[:, c, :], s_ln3[:, c * 128:(c + 1) * 128],
                     transpose=True)
    dma("sync", sb_ebb.ap(), d_eb.broadcast_to([128, 4]))
    W("sync", t_st_ac)
    for hc in range(NCH):
        dma("sync", sb_act.ap()[:, hc, :], s_acat[:, hc * 128:(hc + 1) * 128],
            transpose=True)
    t_acT = cur("qs")
    prev_bc = None
    for hc in range(NCH):
        src = d_lb[:, hc * 128:(hc + 1) * 128, :].rearrange("e p r -> p e r")
        t_l = dma("gpsimd", bctT_sb.ap().rearrange("p (e r) -> p e r", e=4), src)
        W("tensor", t_l)
        if prev_bc is not None:
            W("tensor", prev_bc)
        t_t = tr(t_bf[0][0:32, 0:128], bctT_sb.ap(), track=True)
        W("vector", t_t)
        t_cp = ve(lambda e, hc=hc: e.tensor_copy(sb_bct.ap()[:, hc * 128:(hc + 1) * 128],
                                                 t_bf[0][0:32, 0:128]))
        prev_bc = t_cp
        if hc + 1 < NCH:
            W("gpsimd", t_t)
    zl = zl_sb.ap()
    zm_done = None
    z_ev = None
    for tc in range(TTILES):
        W("tensor", t_ln3f if tc == 0 else None)
        W("tensor", t_acT if tc == 0 else None)
        W("tensor", z_ev)
        for kc in range(NCH):
            trk = (kc == NCH - 1)
            t_z = mm(z_ps, ln3f.ap()[:, kc, tc * 128:(tc + 1) * 128],
                     sb_act.ap()[:, kc, 0:36], start=(kc == 0), stop=trk, track=trk)
        W("vector", t_z)
        W("vector", cur("qs") if tc == 0 else None)
        c1 = ve(lambda e: e.scalar_tensor_tensor(zl[:, 0:4], z_ps[:, 32:36], 1.0,
                                                 sb_ebb.ap(), op0=ALU.mult, op1=ALU.add))
        W("vector", c1)
        c2 = ve(lambda e: e.reduce_max(zl[:, 4:5], zl[:, 0:4], axis=AX))
        W("vector", c2)
        c3 = ve(lambda e: e.tensor_single_scalar(zl[:, 8:12], zl[:, 0:4], zl[:, 4:5],
                                                 op=ALU.is_ge))
        W("vector", c3)
        for ex in range(4):
            t_zm = ve(lambda e, ex=ex: e.tensor_scalar_mul(
                zm_sb.ap()[:, ex * 8:(ex + 1) * 8], z_ps[:, ex * 8:(ex + 1) * 8],
                zl[:, 8 + ex:9 + ex]))
        z_ev = t_zm
        W("tensor", t_zm)
        W("tensor", zm_done)
        t_zt = tr(t_bf[1][0:32, 0:128], zm_sb.ap(), track=True)
        W("vector", t_zt)
        zm_done = ve(lambda e, tc=tc: e.tensor_copy(sb_zmt.ap()[:, tc * 128:(tc + 1) * 128],
                                                    t_bf[1][0:32, 0:128]))
    if DEBUG:
        W("sync", zm_done)
        dma("sync", s_dzmt, sb_zmt.ap())
        dma("sync", s_dbct, sb_bct.ap())
        dma("sync", s_dact, sb_act.ap().rearrange("p c m -> p (c m)"))
        dma("sync", s_dzl, zl_sb.ap())
    W("scalar", t_st_wu)
    W("scalar", t_acT)
    st_y = []
    bd_ev = {}
    t_wu = None
    wu_pe_done = {}

    def load_wu(mg):
        for kc in range(NCH):
            tok = dma("scalar", wup_blk.ap()[:, kc, :],
                      s_wub[mg * 512:(mg + 1) * 512, kc * 128:(kc + 1) * 128],
                      transpose=True)
        return tok

    t_wu = load_wu(0)
    for mg in range(8):
        W("tensor", t_wu)
        W("tensor", zm_done if mg <= 1 else None)
        stop4 = {}
        for mcl in range(4):
            mc = mg * 4 + mcl
            for nb in range(NBLK):
                slot = (mg % 2) * 4 + ((mcl + nb) % 4)
                W("tensor", bd_ev.get(slot))
                for kc in range(NCH):
                    mm(ps8[slot], wup_blk.ap()[:, kc, mcl * 128:(mcl + 1) * 128],
                       ln3f.ap()[:, kc, nb * 512:(nb + 1) * 512],
                       start=(kc == 0), stop=False)
                t_bd = mm(ps8[slot], sb_bct.ap()[:, mc * 128:(mc + 1) * 128],
                          sb_zmt.ap()[:, nb * 512:(nb + 1) * 512],
                          start=False, stop=True, track=True)
                stop4[(mcl, nb)] = (slot, t_bd)
                W("vector", t_bd)
                if len(st_y) >= 8:
                    W("vector", st_y[len(st_y) - 8])
                sg = stg6[len(st_y) % 8].ap()
                t_ev = ve(lambda e, s=slot, o=sg: e.tensor_copy(o, ps8[s]))
                bd_ev[slot] = t_ev
                W("sync", t_ev)
                st_y.append(dma("sync", s_y[mc, :, nb * 512:(nb + 1) * 512], sg))
        if mg + 1 < 8:
            W("scalar", stop4[(3, NBLK - 1)][1])
            t_wu = load_wu(mg + 1)
    barrier()

    W("sync", st_y[-1])
    y_cols = s_y.rearrange("c p t -> (c p) t")
    t_yt = [None, None]
    t_x7 = [None, None]
    t_yt[0] = dma("sync", wot_sb[0].ap(), y_cols[:, 0:128], transpose=True)
    W("scalar", st5[-2])
    t_x7[0] = dma("scalar", xt_sb[0].ap(), s_x1[0:128, :])
    st7 = []
    prev7 = None
    prev8 = None
    for i in range(TTILES):
        b = i % 2
        if i + 1 < TTILES:
            W("sync", prev7)
            if i >= 1:
                W("scalar", st7[i - 1])
            t_yt[(i + 1) % 2] = dma("sync", wot_sb[(i + 1) % 2].ap(),
                                    y_cols[:, (i + 1) * 128:(i + 2) * 128], transpose=True)
            t_x7[(i + 1) % 2] = dma("scalar", xt_sb[(i + 1) % 2].ap(),
                                    s_x1[(i + 1) * 128:(i + 2) * 128, :])
        W("vector", t_yt[b])
        if i >= 2:
            W("vector", st7[i - 2])
        if prev7 is not None:
            W("vector", prev7)
        _, t7 = ln_stats_apply(wot_sb[b].ap(), 12, f32a_sb.ap())
        prev7 = t7
        W("vector", t7)
        W("vector", t_x7[b])
        if prev8 is not None:
            W("scalar", prev8)
        t8 = ve(lambda e, b=b: e.tensor_add(f32b_sb[b].ap(), xt_sb[b].ap(), f32a_sb.ap()))
        prev8 = t8
        W("sync", t8)
        st7.append(dma("sync", d_out[i * 128:(i + 1) * 128, :], f32b_sb[b].ap()))
    W("sync", cur("qs"))
    W("gpsimd", cur("qg"))
    W("scalar", cur("qa"))

    sem_names = ["bar", "qs", "qa", "qg", "pe", "ve", "ac", "gp"]
    for n in sem_names:
        S[n] = nc.alloc_semaphore(n)

    with nc.Block() as block:
        @block.sync
        def _(eng):
            for f in ops["sync"]:
                f(eng)

        @block.tensor
        def _(eng):
            for f in ops["tensor"]:
                f(eng)

        @block.vector
        def _(eng):
            for f in ops["vector"]:
                f(eng)

        @block.scalar
        def _(eng):
            for f in ops["scalar"]:
                f(eng)

        @block.gpsimd
        def _(eng):
            for f in ops["gpsimd"]:
                f(eng)

    return nc


_cache = {}


def kernel(L0_attn_hidden_states, L0_attn_Wq, L0_attn_Wk, L0_attn_Wv, L0_attn_Wo,
           L0_ffn_W_up, L0_ffn_gate_w, L0_ffn_expert_bias, L0_ffn_lora_A, L0_ffn_lora_B):
    cos_t, sincross, ident, mask = _consts()
    cstv = np.broadcast_to(
        np.array([EPS, -0.5, 1.0 / D, SCL, 0.0, 0.0, 0.0, 0.0], dtype=np.float32),
        (128, 8)).copy()
    x = np.ascontiguousarray(np.asarray(L0_attn_hidden_states, dtype=np.float32))
    wq = np.ascontiguousarray(np.asarray(L0_attn_Wq, dtype=np.float32))
    wk = np.ascontiguousarray(np.asarray(L0_attn_Wk, dtype=np.float32))
    wv = np.ascontiguousarray(np.asarray(L0_attn_Wv, dtype=np.float32))
    wo = np.ascontiguousarray(np.asarray(L0_attn_Wo, dtype=np.float32))
    wup = np.ascontiguousarray(np.asarray(L0_ffn_W_up, dtype=np.float32))
    gw = np.ascontiguousarray(np.asarray(L0_ffn_gate_w, dtype=np.float32))
    eb = np.ascontiguousarray(np.asarray(L0_ffn_expert_bias, dtype=np.float32)).reshape(1, 4)
    la = np.ascontiguousarray(np.asarray(L0_ffn_lora_A, dtype=np.float32)).reshape(32, D)
    lb = np.ascontiguousarray(np.asarray(L0_ffn_lora_B, dtype=np.float32))

    nc = build_program()
    in_maps = []
    for b in range(8):
        in_maps.append({
            "x": np.ascontiguousarray(x[b]),
            "wq": wq, "wk": wk, "wv": wv, "wo": wo, "wup": wup,
            "gw": gw, "eb": eb, "la": la, "lb": lb,
            "cos_t": cos_t, "sincross": sincross, "ident": ident, "maskfd": mask,
            "cstv": cstv,
        })
    res = run_bass_kernel_spmd(nc, in_maps, core_ids=list(range(8)))
    out = np.stack([np.asarray(r["out"]) for r in res.results], axis=0)
    return out.astype(np.float32)


if __name__ == "__main__":
    import os
    import jax
    cpu = jax.local_devices(backend="cpu")[0]
    import reference as refmod
    with jax.default_device(cpu):
        inp = refmod.setup_inputs()
        inp = {k: np.asarray(v) for k, v in inp.items()}
    out = kernel(**inp)
    with jax.default_device(cpu):
        ref = np.asarray(refmod.reference(**inp))
    num = np.linalg.norm((out - ref).ravel())
    den = np.linalg.norm(ref.ravel())
    print("rel err:", num / den)
